# version 02 (final)
# baseline (speedup 1.0000x reference)
import math
import contextlib
import numpy as np
import ml_dtypes
import concourse.bass as bass
import concourse.mybir as mybir
from concourse.bass_utils import run_bass_kernel_spmd

F32 = mybir.dt.float32
BF16 = mybir.dt.bfloat16
AF = mybir.ActivationFunctionType
ALU = mybir.AluOpType
AX = mybir.AxisListType
BF = ml_dtypes.bfloat16

S = 4096
D = 4096
NH = 16
DH = 128
HS = NH * DH
WIN = 1024
NWIN = S // WIN
PCW = WIN // 512
NSLAB = 8
SLAB = 512
MYR = 2048
EPS = 1e-5
QSC = 1.0 / math.sqrt(DH)
CPAD = math.log(1.02 / math.sqrt(DH))

ENGS = ["sync", "gpsimd", "tensor", "vector", "scalar"]
PHB_PARTS = {"cs", "ab", "xn", "qk", "v"}


def _host_constants():
    pos = np.arange(S, dtype=np.float64)
    inv_freq = 1.0 / (10000.0 ** (np.arange(0, DH, 2, dtype=np.float64) / DH))
    freqs = pos[None, :] * inv_freq[(np.arange(DH) % 64), None]
    cos = np.cos(freqs)
    sin = np.sin(freqs)
    sin_eff = np.where(np.arange(DH)[:, None] < 64, -sin, sin)
    kk = np.arange(128)[:, None]
    qq = np.arange(512)[None, :]
    masks = np.concatenate([(d * 128 + kk <= qq).astype(BF) for d in range(4)], axis=1)
    return {
        "cosq": (cos * QSC).astype(BF), "sinq": (sin_eff * QSC).astype(BF),
        "cosk": cos.astype(BF), "sink": sin_eff.astype(BF),
        "masks": masks,
        "ones_col": np.ones((128, 1), dtype=BF),
        "ones_row": np.ones((1, 128), dtype=BF),
        "ones_rowf": np.ones((128, 128), dtype=np.float32),
        "ident": np.eye(128, dtype=np.float32),
        "identb": np.eye(128, dtype=BF),
        "sconst": np.tile(np.array([[0.0, EPS, CPAD]], dtype=np.float32), (128, 1)),
    }


class Ctr:
    __slots__ = ("h", "n")

    def __init__(self, handle):
        self.h = handle
        self.n = 0


class Emitter:

    CHAINED = ("vector", "scalar")

    def __init__(self, nc):
        self.nc = nc
        self.eng = None
        self.engname = None
        self.ctrs = []
        self.last_inc = {}
        self.chains = {}

    def sem(self, stack, name):
        c = Ctr(stack.enter_context(self.nc.semaphore(name)))
        self.ctrs.append(c)
        return c

    def snapshot(self):
        return [c.n for c in self.ctrs]

    def restore(self, snap):
        for c, v in zip(self.ctrs, snap):
            c.n = v

    def op(self, engname, fn, waits=(), inc=None, k=1):
        waits = [w for w in waits if w is not None and w[1] > 0]
        if engname in self.chains:
            inc = self.chains[engname]
            k = 1
            waits = waits + [(inc, inc.n)]
        if inc is not None:
            if k >= 16 and inc.n > 0:
                waits = waits + [(inc, inc.n)]
            inc.n += k
            self.last_inc[engname] = (inc, inc.n)
        if self.engname == engname and self.eng is not None:
            for c, v in waits:
                self.eng.wait_ge(c.h, v)
            ins = fn(self.eng)
            if inc is not None and ins is not None:
                ins.then_inc(inc.h, k)
        return (inc, inc.n) if inc is not None else None

    def wait(self, engname, waits):
        if self.engname == engname and self.eng is not None:
            for w in waits:
                if w is not None and w[1] > 0:
                    self.eng.wait_ge(w[0].h, w[1])

    def run_phase(self, name, body):
        snap = self.snapshot()
        li = dict(self.last_inc)
        with self.nc.Block(name) as block:
            for en in ENGS:
                self.restore(snap)
                self.last_inc = dict(li)
                self.engname = en

                def f(engobj, _b=body):
                    self.eng = engobj
                    _b()
                    self.eng = None

                getattr(block, en)(f)
        self.restore(snap)
        self.last_inc = dict(li)
        self.engname = None
        body()


def build_nc(debug=False, phases=5):
    nc = bass.Bass("TRN2", target_bir_lowering=False)
    C = _host_constants()

    xT = nc.dram_tensor("xT", [D, S], F32, kind="ExternalInput").ap()
    xrow = nc.dram_tensor("xrow", [S, D], F32, kind="ExternalInput").ap()
    x_res = nc.dram_tensor("x_res", [MYR, D], F32, kind="ExternalInput").ap()
    xT_ffn = nc.dram_tensor("xT_ffn", [D, MYR], F32, kind="ExternalInput").ap()
    wq = nc.dram_tensor("wq", [HS, D], F32, kind="ExternalInput").ap()
    wk = nc.dram_tensor("wk", [HS, D], F32, kind="ExternalInput").ap()
    wv = nc.dram_tensor("wv", [1024, 8192], F32, kind="ExternalInput").ap()
    wo = nc.dram_tensor("wo", [1024, 8192], F32, kind="ExternalInput").ap()
    up = nc.dram_tensor("up", [2048, 2048], F32, kind="ExternalInput").ap()
    down = nc.dram_tensor("down", [2048, 2048], F32, kind="ExternalInput").ap()
    out = nc.dram_tensor("out", [MYR, D], F32, kind="ExternalOutput").ap()

    wq_bf = nc.dram_tensor("wq_bf", [HS, D], BF16).ap()
    wk_bf = nc.dram_tensor("wk_bf", [HS, D], BF16).ap()
    wv_bf = nc.dram_tensor("wv_bf", [1024, 8192], BF16).ap()
    wo_bf = nc.dram_tensor("wo_bf", [1024, 8192], BF16).ap()
    up_bf = nc.dram_tensor("up_bf", [2048, 2048], BF16).ap()
    down_bf = nc.dram_tensor("down_bf", [2048, 2048], BF16).ap()
    qT_d = nc.dram_tensor("qT_d", [HS, S], BF16).ap()
    kT_d = nc.dram_tensor("kT_d", [HS, S], BF16).ap()
    v_d = nc.dram_tensor("v_d", [128, NH * S], BF16).ap()
    ffn_d = nc.dram_tensor("ffn_d", [MYR, D], BF16).ap()
    rs_in_s = [nc.dram_tensor(f"rs_in{i}", [SLAB, D], BF16).ap() for i in range(NSLAB)]
    rs_out_s = [nc.dram_tensor(f"rs_out{i}", [SLAB // 2, D], BF16).ap()
                for i in range(NSLAB)]

    cosq_c = nc.inline_tensor(C["cosq"], "cosq_c").ap()
    sinq_c = nc.inline_tensor(C["sinq"], "sinq_c").ap()
    cosk_c = nc.inline_tensor(C["cosk"], "cosk_c").ap()
    sink_c = nc.inline_tensor(C["sink"], "sink_c").ap()
    masks_c = nc.inline_tensor(C["masks"], "masks_c").ap()
    onescol_c = nc.inline_tensor(C["ones_col"], "onescol_c").ap()
    onesrow_c = nc.inline_tensor(C["ones_row"], "onesrow_c").ap()
    onesrowf_c = nc.inline_tensor(C["ones_rowf"], "onesrowf_c").ap()
    ident_c = nc.inline_tensor(C["ident"], "ident_c").ap()
    identb_c = nc.inline_tensor(C["identb"], "identb_c").ap()
    sconst_c = nc.inline_tensor(C["sconst"], "sconst_c").ap()

    st = contextlib.ExitStack()
    E = Emitter(nc)
    for _en in Emitter.CHAINED:
        E.chains[_en] = E.sem(st, f"s_chain_{_en}")

    dbg = {}
    if debug:
        for nm, shape, dt in [
            ("dbg_rows", [128, S], F32), ("dbg_xnT", [128, 32 * WIN], BF16),
            ("dbg_wq", [128, 4096], BF16), ("dbg_ab", [128, 2 * WIN], F32),
            ("dbg_xb", [128, WIN], F32), ("dbg_m2", [1, 512], F32),
            ("dbg_cb", [128, NH], F32), ("dbg_ch", [1, 4 * NH], F32),
            ("dbg_qT", [HS, S], BF16), ("dbg_kT", [HS, S], BF16),
            ("dbg_v", [128, NH * S], BF16), ("dbg_ffn", [MYR, D], BF16),
            ("dbg_rsin", [SLAB, D], BF16),
        ]:
            dbg[nm] = nc.dram_tensor(nm, shape, dt, kind="ExternalOutput").ap()

    rows_sb = st.enter_context(nc.sbuf_tensor("rows_sb", [128, S], F32))
    bfrow = st.enter_context(nc.sbuf_tensor("bfrow", [1, MYR], F32))
    m2parts = st.enter_context(nc.sbuf_tensor("m2parts", [1, 512], F32))
    cbias = st.enter_context(nc.sbuf_tensor("cbias", [128, NH], F32))
    onescol = st.enter_context(nc.sbuf_tensor("onescol", [128, 1], BF16))
    onesrow = st.enter_context(nc.sbuf_tensor("onesrow", [1, 128], BF16))
    onesrowf = st.enter_context(nc.sbuf_tensor("onesrowf", [128, 128], F32))
    ident = st.enter_context(nc.sbuf_tensor("ident", [128, 128], F32))
    identb = st.enter_context(nc.sbuf_tensor("identb", [128, 128], BF16))
    masks = st.enter_context(nc.sbuf_tensor("masks", [128, 4 * 512], BF16))
    sconst = st.enter_context(nc.sbuf_tensor("sconst", [128, 3], F32))
    junk = st.enter_context(nc.sbuf_tensor("junk", [1, 16], F32))

    arow = rows_sb[0:1, :]
    brow = rows_sb[32:33, :]
    afrow = rows_sb[64:65, 0:MYR]

    sem = lambda name: E.sem(st, name)
    sems = lambda name, n: [E.sem(st, f"{name}{i}") for i in range(n)]
    s_const = sem("s_const")
    s_cast = {nm: sem(f"s_cast_{nm}") for nm in ["wq", "wk", "wv", "wo", "up", "down"]}
    s_ph = sem("s_ph")
    s_x0p = sems("s_x0", 2)
    s_v0 = sem("s_v0")
    s_s0 = sem("s_s0")
    s_t0 = sem("s_t0")
    s_r0 = sem("s_r0")
    s_xTp = sems("s_xT", 2)
    s_xn = sem("s_xn")
    s_ab = sem("s_ab")
    s_wl = [[sem(f"s_wl{t}{p}") for p in range(2)] for t in range(2)]
    s_pq = sem("s_pq")
    s_qe = sem("s_qe")
    s_swp = sems("s_sw", 2)
    s_rot = sem("s_rot")
    s_qsp = sems("s_qs", 2)
    s_sq = sem("s_sq")
    s_np = sem("s_np")
    s_ne = sem("s_ne")
    s_cs = sem("s_cs")
    s_wv = sem("s_wv")
    s_vp = sem("s_vp")
    s_ve = sem("s_ve")
    s_vs = sem("s_vs")
    s_ch = sem("s_ch")
    s_fxp = sems("s_fx", 2)
    s_fab = sem("s_fab")
    s_fn = sem("s_fn")
    s_fwu = sem("s_fwu")
    s_fwd = sem("s_fwd")
    s_fm = sem("s_fm")
    s_fg = sem("s_fg")
    s_fd = sem("s_fd")
    s_fe = sem("s_fe")
    s_fs = sem("s_fs")
    s_kvp = sems("s_kv", 2)
    s_Sg = sem("s_Sg")
    s_ex = sem("s_ex")
    s_mk = sem("s_mk")
    s_pg = sem("s_pg")
    s_dc = sem("s_dc")
    s_bc = sem("s_bc")
    s_be = sem("s_be")
    s_hT = sem("s_hT")
    s_wo = sem("s_wo")
    s_we = sem("s_we")
    s_wdp = sems("s_wd", 2)
    s_rw = sem("s_rw")
    s_cc = sem("s_cc")
    s_alp = sems("s_al", 3)
    s_av = sem("s_av")
    s_odp = sems("s_od", 2)

    marks = {}
    mget = marks.get

    def barrier():
        E.op("sync", lambda e: e.nop(), inc=s_ph)
        E.op("gpsimd", lambda e: e.memset(junk[0:1, 4:5], 0.0), inc=s_ph)
        return ((s_ph, s_ph.n), E.last_inc.get("tensor"),
                (E.chains["vector"], E.chains["vector"].n),
                (E.chains["scalar"], E.chains["scalar"].n))

    def phase_wait(bar):
        for en in ENGS:
            E.wait(en, list(bar))

    pA = st.enter_context(contextlib.ExitStack())
    xr_b = [pA.enter_context(nc.sbuf_tensor(f"xr{i}", [128, D], F32)) for i in range(2)]
    stats6 = pA.enter_context(nc.sbuf_tensor("stats6", [128, 48], F32))
    mv = pA.enter_context(nc.sbuf_tensor("mv", [128, 2], F32))
    abcols = pA.enter_context(nc.sbuf_tensor("abcols", [128, 2], F32))
    tmp1s = pA.enter_context(nc.sbuf_tensor("tmp1s", [128, 1], F32))
    tmp1v = pA.enter_context(nc.sbuf_tensor("tmp1v", [128, 1], F32))
    msum = pA.enter_context(nc.sbuf_tensor("msum", [128, 1], F32))
    abhl = pA.enter_context(nc.sbuf_tensor("abhl", [128, 4], BF16))
    ptr_ab = pA.enter_context(nc.psum_tensor("ptr_ab", [1, 384], F32))

    def phaseA():
        for apc, dst in [(onescol_c, onescol[:]), (onesrow_c, onesrow[:]),
                         (onesrowf_c, onesrowf[:]), (ident_c, ident[:]),
                         (masks_c, masks[:]), (sconst_c, sconst[:]),
                         (identb_c, identb[:])]:
            E.op("sync", lambda e, a=apc, d=dst: e.dma_start(out=d, in_=a),
                 inc=s_const, k=16)
        for name, src, dst in [("wq", wq, wq_bf), ("wk", wk, wk_bf),
                               ("wv", wv, wv_bf), ("wo", wo, wo_bf),
                               ("up", up, up_bf), ("down", down, down_bf)]:
            n = src.shape[0]
            step = n // 4
            for r in range(4):
                marks[("cast", name)] = E.op(
                    "gpsimd",
                    lambda e, s2=src, d2=dst, r0=r * step, r1=(r + 1) * step:
                        e.dma_start(out=d2[r0:r1, :], in_=s2[r0:r1, :]),
                    inc=s_cast[name], k=16)

        def pkey(name, part, t):
            if t >= 0:
                return mget((name, part, t))
            if part == 1:
                return mget((name, 0, 32 + t))
            return None

        for part in range(2):
            src = xrow if part == 0 else x_res
            nt = 32 if part == 0 else 16
            dsta, dstb = (arow, brow) if part == 0 else (afrow, bfrow[0:1, :])
            for t in range(nt):
                buf = xr_b[t % 2]
                marks[("x0", part, t)] = E.op(
                    "sync", lambda e, b=buf, s2=src, t0=t * 128:
                        e.dma_start(out=b[:], in_=s2[t0:t0 + 128, :]),
                    waits=[pkey("bn", part, t - 2)],
                    inc=s_x0p[t % 2], k=16)
                E.wait("vector", [marks[("x0", part, t)], pkey("rstd", part, t - 1)])
                for i in range(8):
                    marks[("bns", part, t)] = E.op(
                        "vector", lambda e, b=buf, i=i:
                            e.bn_stats(stats6[:, 6 * i:6 * i + 6],
                                       b[:, 512 * i:512 * i + 512]),
                        inc=s_v0 if i == 7 else None)
                E.wait("vector", [marks[("bns", part, t)]])
                E.op("vector", lambda e: e.bn_aggr(mv[:], stats6[:]))
                marks[("bn", part, t)] = E.op(
                    "vector", lambda e, b=buf: e.reduce_sum(msum[:], b[:], axis=AX.X),
                    inc=s_v0)
                E.wait("scalar", [marks[("bn", part, t)], pkey("tr", part, t - 1),
                                  (s_const, 112)])
                E.op("scalar", lambda e: e.activation(tmp1s[:], mv[:, 1:2], AF.Ln,
                                                      bias=sconst[:, 1:2], scale=1.0))
                marks[("rstd", part, t)] = E.op(
                    "scalar", lambda e: e.activation(abcols[:, 0:1], tmp1s[:], AF.Exp,
                                                     bias=sconst[:, 0:1], scale=-0.5),
                    inc=s_s0)
                E.wait("vector", [marks[("rstd", part, t)], marks[("bn", part, t)],
                                  pkey("tr", part, t - 1)])
                E.op("vector", lambda e: e.tensor_mul(tmp1v[:], msum[:],
                                                      abcols[:, 0:1]))
                E.op("vector", lambda e: e.tensor_scalar_mul(abcols[:, 1:2],
                                                             tmp1v[:], -1.0 / D))
                marks[("bcol", part, t)] = E.op(
                    "vector", lambda e: e.memset(junk[0:1, 10:11], 0.0),
                    inc=s_v0)
                E.wait("gpsimd", [marks[("bcol", part, t)]])
                E.op("gpsimd", lambda e, d2=dsta, t0=t * 128: e.dma_start(
                    out=d2[0:1, t0:t0 + 128], in_=abcols[:, 0:1]),
                    inc=s_t0, k=16)
                marks[("tr", part, t)] = E.op(
                    "gpsimd", lambda e, d2=dstb, t0=t * 128: e.dma_start(
                        out=d2[0:1, t0:t0 + 128], in_=abcols[:, 1:2]),
                    inc=s_t0, k=16)
                marks[("rowcp", part, t)] = E.op(
                    "gpsimd", lambda e: e.memset(junk[0:1, 8:9], 0.0),
                    waits=[marks[("tr", part, t)]], inc=s_r0)
        marks["barA"] = barrier()

    E.run_phase("phA", phaseA)
    pA.close()

    pB = st.enter_context(contextlib.ExitStack())
    xnT_win = pB.enter_context(nc.sbuf_tensor("xnT_win", [128, 32 * WIN], BF16))
    xb_b = [pB.enter_context(nc.sbuf_tensor(f"xb{i}", [128, WIN], F32)) for i in range(2)]
    ttmp = pB.enter_context(nc.sbuf_tensor("ttmp", [128, WIN], F32))
    ab_sb = pB.enter_context(nc.sbuf_tensor("ab_sb", [128, 2 * WIN], F32))
    wqh_b = [pB.enter_context(nc.sbuf_tensor(f"wqh{i}", [128, 4096], BF16)) for i in range(2)]
    wkh_b = [pB.enter_context(nc.sbuf_tensor(f"wkh{i}", [128, 4096], BF16)) for i in range(2)]
    qraw = pB.enter_context(nc.sbuf_tensor("qraw", [128, WIN], BF16))
    qswap = pB.enter_context(nc.sbuf_tensor("qswap", [128, WIN], BF16))
    rt2 = pB.enter_context(nc.sbuf_tensor("rt2", [128, WIN], BF16))
    rt3 = pB.enter_context(nc.sbuf_tensor("rt3", [128, WIN], BF16))
    qstage = [pB.enter_context(nc.sbuf_tensor(f"qstage{i}", [128, WIN], BF16)) for i in range(2)]
    qsq = pB.enter_context(nc.sbuf_tensor("qsq", [128, WIN], BF16))
    cs_sb = pB.enter_context(nc.sbuf_tensor("cs_sb", [128, 4 * WIN], BF16))
    wvc = pB.enter_context(nc.sbuf_tensor("wvc", [128, 32 * 256], BF16))
    vstage = pB.enter_context(nc.sbuf_tensor("vstage", [128, 2 * 8 * 128], BF16))
    ps_qk = [pB.enter_context(nc.psum_tensor(f"ps_qk{i}", [128, 512], F32)) for i in range(2)]
    ps_v = [pB.enter_context(nc.psum_tensor(f"ps_v{i}", [128, 256], F32)) for i in range(2)]
    ps_ab = pB.enter_context(nc.psum_tensor("ps_ab", [128, 512], F32))
    ps_nrm = pB.enter_context(nc.psum_tensor("ps_nrm", [1, 512], F32))

    def phaseB():
        phase_wait(marks["barA"])
        qk_idx = 0
        v_idx = 0
        sq_idx = 0

        def prev_ht(w, h, ti):
            if ti == 1:
                return (w, h, 0)
            if h > 0:
                return (w, h - 1, 1)
            if w > 0:
                return (w - 1, NH - 1, 1)
            return None

        def mprev(name, w, h, ti):
            k2 = prev_ht(w, h, ti)
            return mget((name,) + k2) if k2 else None

        def mprev_ti(name, w, h, ti):
            if h > 0:
                return mget((name, w, h - 1, ti))
            if w > 0:
                return mget((name, w - 1, NH - 1, ti))
            return None

        def mprev2_ti(name, w, h, ti):
            if h >= 2:
                return mget((name, w, h - 2, ti))
            if w > 0:
                return mget((name, w - 1, NH + h - 2, ti))
            return None

        for w in range(NWIN):
            P0 = w * WIN
            for j, capn in enumerate([cosq_c, sinq_c, cosk_c, sink_c]
                                     if "cs" in PHB_PARTS else []):
                marks[("cs", w, j)] = E.op(
                    "sync", lambda e, a=capn, j=j: e.dma_start(
                        out=cs_sb[:, j * WIN:(j + 1) * WIN], in_=a[:, P0:P0 + WIN]),
                    waits=[mget(("rotall", w - 1))],
                    inc=s_cs, k=16)
            for j, (rsrc, rbase) in enumerate([(arow, 0), (brow, 32)]
                                              if "ab" in PHB_PARTS else []):
                for n2 in range(2):
                    marks[("abps", w, j, n2)] = E.op(
                        "tensor", lambda e, r=rsrc, rb=rbase, n2=n2: e.matmul(
                            ps_ab[:], onesrowf[rb:rb + 1, :],
                            r[0:1, P0 + 512 * n2:P0 + 512 * (n2 + 1)],
                            start=True, stop=True),
                        waits=[mget(("rowcp", 0, 31)) if (w == 0 and j == 0 and n2 == 0) else None,
                               (s_const, 112),
                               mget(("abcp", w, j, n2 - 1)) if n2 else
                               mget(("abcp", w, j - 1, 1)) if j else
                               mget(("abcp", w - 1, 1, 1))],
                        inc=s_ab)
                    E.wait("vector", [marks[("abps", w, j, n2)]])
                    marks[("abcp", w, j, n2)] = E.op(
                        "vector", lambda e, j=j, n2=n2: e.tensor_copy(
                            ab_sb[:, j * WIN + 512 * n2:j * WIN + 512 * (n2 + 1)],
                            ps_ab[:]),
                        inc=s_ab)
            for ci in (range(32) if "xn" in PHB_PARTS else []):
                buf = xb_b[ci % 2]
                marks[("xT", w, ci)] = E.op(
                    "sync", lambda e, b=buf, ci=ci: e.dma_start(
                        out=b[:], in_=xT[128 * ci:128 * ci + 128, P0:P0 + WIN]),
                    waits=[mget(("xn", w, ci - 2)) if ci >= 2 else
                           mget(("xn", w - 1, 30 + ci))],
                    inc=s_xTp[ci % 2], k=16)
                E.wait("vector", [marks[("xT", w, ci)], mget(("abcp", w, 1, 1))])
                if w > 0 and ci == 0:
                    E.wait("vector", [mget(("pqall", w - 1)), mget(("vpall", w - 1))])
                E.op("vector", lambda e, b=buf: e.tensor_mul(
                    ttmp[:], b[:], ab_sb[:, 0:WIN]))
                marks[("xn", w, ci)] = E.op(
                    "vector", lambda e, ci=ci: e.tensor_add(
                        xnT_win[:, WIN * ci:WIN * (ci + 1)], ttmp[:],
                        ab_sb[:, WIN:2 * WIN]),
                    inc=s_xn)
            marks[("xnall", w)] = mget(("xn", w, 31))
            for h in (range(NH) if "qk" in PHB_PARTS else []):
                for ti, (wsrc, wbuf, cofs, stg_d) in enumerate(
                        [(wq_bf, wqh_b, 0, qT_d), (wk_bf, wkh_b, 2 * WIN, kT_d)]):
                    buf = wbuf[h % 2]
                    marks[("wl", w, h, ti)] = E.op(
                        "sync", lambda e, b=buf, s2=wsrc, h=h: e.dma_start(
                            out=b[:], in_=s2[128 * h:128 * h + 128, :]),
                        waits=[mget(("cast", "wq" if ti == 0 else "wk")),
                               mprev2_ti("pqh", w, h, ti)],
                        inc=s_wl[ti][h % 2], k=16)
                    mvv = None
                    for pc in range(PCW):
                        ps = ps_qk[qk_idx % 2]
                        E.wait("tensor", [marks[("wl", w, h, ti)],
                                          marks[("xnall", w)],
                                          mget(("qe", qk_idx - 2))])
                        for ci in range(32):
                            mvv = E.op(
                                "tensor", lambda e, b=buf, ps=ps, ci=ci, pc=pc:
                                    e.matmul(ps[:], b[:, 128 * ci:128 * ci + 128],
                                             xnT_win[:, WIN * ci + 512 * pc:
                                                     WIN * ci + 512 * (pc + 1)],
                                             start=(ci == 0), stop=(ci == 31)),
                                inc=s_pq if ci == 31 else None)
                        marks[("pq", qk_idx)] = mvv
                        E.wait("vector", [marks[("pq", qk_idx)],
                                          mprev("swap", w, h, ti) if pc == 0 else None])
                        marks[("qe", qk_idx)] = E.op(
                            "vector", lambda e, ps=ps, pc=pc: e.tensor_copy(
                                qraw[:, 512 * pc:512 * (pc + 1)], ps[:]),
                            inc=s_qe)
                        qk_idx += 1
                    marks[("pqh", w, h, ti)] = marks[("pq", qk_idx - 1)]
                    E.wait("sync", [marks[("qe", qk_idx - 1)],
                                    mprev("rot", w, h, ti)])
                    E.op("sync", lambda e: e.dma_start(
                        out=qswap[0:64, :], in_=qraw[64:128, :]), inc=s_swp[0], k=16)
                    marks[("swap", w, h, ti)] = E.op(
                        "sync", lambda e: e.dma_start(
                            out=qswap[64:128, :], in_=qraw[0:64, :]),
                        inc=s_swp[1], k=16)
                    stg = qstage[ti]
                    E.wait("vector", [(s_swp[0], marks[("swap", w, h, ti)][1]),
                                      marks[("swap", w, h, ti)],
                                      mget(("cs", w, 3)),
                                      mprev_ti("qs", w, h, ti),
                                      mprev_ti("sq", w, h, ti)])
                    E.op("vector", lambda e, c=cofs: e.tensor_mul(
                        rt2[:], qraw[:], cs_sb[:, c:c + WIN]))
                    E.op("vector", lambda e, c=cofs: e.tensor_mul(
                        rt3[:], qswap[:], cs_sb[:, c + WIN:c + 2 * WIN]))
                    marks[("rot", w, h, ti)] = E.op(
                        "vector", lambda e, g=stg: e.tensor_add(g[:], rt2[:], rt3[:]),
                        inc=s_rot)
                    E.wait("sync", [marks[("rot", w, h, ti)]])
                    marks[("qs", w, h, ti)] = E.op(
                        "sync", lambda e, g=stg, d2=stg_d, h=h: e.dma_start(
                            out=d2[128 * h:128 * h + 128, P0:P0 + WIN], in_=g[:]),
                        inc=s_qsp[ti], k=16)
                    E.wait("scalar", [marks[("rot", w, h, ti)]])
                    marks[("sq", w, h, ti)] = E.op(
                        "scalar", lambda e, g=stg: e.activation(
                            qsq[:], g[:], AF.Square, bias=sconst[:, 0:1], scale=1.0),
                        inc=s_sq)
                    for n2 in range(PCW):
                        E.wait("tensor", [marks[("sq", w, h, ti)], (s_const, 112),
                                          mget(("ne", sq_idx - 1))])
                        marks[("np", sq_idx)] = E.op(
                            "tensor", lambda e, n2=n2: e.matmul(
                                ps_nrm[:], onescol[:],
                                qsq[:, 512 * n2:512 * (n2 + 1)],
                                start=True, stop=True),
                            inc=s_np)
                        slot = ((h * 2 + ti) * NWIN + w) * PCW + n2
                        E.wait("vector", [marks[("np", sq_idx)]])
                        marks[("ne", sq_idx)] = E.op(
                            "vector", lambda e, sl=slot: e.reduce_max(
                                m2parts[0:1, sl:sl + 1], ps_nrm[:], axis=AX.X),
                            inc=s_ne)
                        sq_idx += 1
            marks[("rotall", w)] = mget(("rot", w, NH - 1, 1))
            marks[("pqall", w)] = (s_pq, s_pq.n)
            for cg in (range(8) if "v" in PHB_PARTS else []):
                marks[("wv", w, cg)] = E.op(
                    "sync", lambda e, cg=cg: e.dma_start(
                        out=wvc[:], in_=wv_bf[128 * cg:128 * (cg + 1), :]),
                    waits=[mget(("cast", "wv")),
                           mget(("vpg", w, cg - 1)) if cg else mget(("vpg", w - 1, 7))],
                    inc=s_wv, k=16)
                mvv = None
                for pt in range(8):
                    ps = ps_v[v_idx % 2]
                    E.wait("tensor", [marks[("wv", w, cg)],
                                      marks[("xnall", w)],
                                      mget(("ve", v_idx - 2))])
                    for ci in range(32):
                        mvv = E.op(
                            "tensor", lambda e, ps=ps, ci=ci, pt=pt: e.matmul(
                                ps[:], xnT_win[:, WIN * ci + 128 * pt:
                                               WIN * ci + 128 * (pt + 1)],
                                wvc[:, 256 * ci:256 * (ci + 1)],
                                start=(ci == 0), stop=(ci == 31)),
                            inc=s_vp if ci == 31 else None)
                    marks[("vp", v_idx)] = mvv
                    E.wait("vector", [marks[("vp", v_idx)],
                                      (mget(("vs", w, cg - 1, 1)) if cg else
                                       mget(("vs", w - 1, 7, 1))) if pt == 0 else None])
                    E.op("vector", lambda e, ps=ps, pt=pt: e.tensor_copy(
                        vstage[:, 128 * pt:128 * (pt + 1)], ps[:, 0:128]))
                    marks[("ve", v_idx)] = E.op(
                        "vector", lambda e, ps=ps, pt=pt: e.tensor_copy(
                            vstage[:, 1024 + 128 * pt:1024 + 128 * (pt + 1)],
                            ps[:, 128:256]),
                        inc=s_ve)
                    v_idx += 1
                marks[("vpg", w, cg)] = marks[("vp", v_idx - 1)]
                E.wait("sync", [marks[("ve", v_idx - 1)]])
                for hh in range(2):
                    marks[("vs", w, cg, hh)] = E.op(
                        "sync", lambda e, cg=cg, hh=hh: e.dma_start(
                            out=v_d[:, 4096 * (2 * cg + hh) + 1024 * w:
                                    4096 * (2 * cg + hh) + 1024 * (w + 1)],
                            in_=vstage[:, 1024 * hh:1024 * (hh + 1)]),
                        inc=s_vs, k=16)
            marks[("vpall", w)] = (s_vp, s_vp.n)
        if debug:
            E.op("sync", lambda e: e.dma_start(out=dbg["dbg_xnT"], in_=xnT_win[:]),
                 waits=[mget(("xnall", NWIN - 1))], inc=s_cs, k=16)
            E.op("sync", lambda e: e.dma_start(out=dbg["dbg_ab"], in_=ab_sb[:]),
                 inc=s_cs, k=16)
            E.op("sync", lambda e: e.dma_start(out=dbg["dbg_xb"], in_=xb_b[1][:]),
                 inc=s_cs, k=16)
            E.op("sync", lambda e: e.dma_start(out=dbg["dbg_wq"], in_=wqh_b[1][:]),
                 waits=[mget(("wl", NWIN - 1, NH - 1, 0))], inc=s_cs, k=16)
        marks["barB"] = barrier()

    if phases >= 2:
        E.run_phase("phB", phaseB)
    pB.close()

    pC = st.enter_context(contextlib.ExitStack())
    chtmp = pC.enter_context(nc.sbuf_tensor("chtmp", [1, 4 * NH], F32))
    ps_cb = pC.enter_context(nc.psum_tensor("ps_cb", [128, 1], F32))

    def phaseC():
        phase_wait(marks["barB"])
        NSL = NWIN * PCW
        E.wait("vector", [mget(("ne", 2 * NH * NSL - 1))])
        for h in range(NH):
            E.op("vector", lambda e, h=h: e.reduce_max(
                chtmp[0:1, 4 * h:4 * h + 1],
                m2parts[0:1, (h * 2) * NSL:(h * 2) * NSL + NSL], axis=AX.X))
            marks[("rm2", h)] = E.op(
                "vector", lambda e, h=h: e.reduce_max(
                    chtmp[0:1, 4 * h + 1:4 * h + 2],
                    m2parts[0:1, (h * 2 + 1) * NSL:(h * 2 + 1) * NSL + NSL],
                    axis=AX.X),
                inc=s_ch)
            E.wait("vector", [marks[("rm2", h)]])
            marks[("m2", h)] = E.op(
                "vector", lambda e, h=h: e.tensor_mul(
                    chtmp[0:1, 4 * h + 2:4 * h + 3],
                    chtmp[0:1, 4 * h:4 * h + 1], chtmp[0:1, 4 * h + 1:4 * h + 2]),
                inc=s_ch)
            E.wait("scalar", [marks[("m2", h)]])
            E.op("scalar", lambda e, h=h: e.activation(
                chtmp[0:1, 4 * h + 3:4 * h + 4], chtmp[0:1, 4 * h + 2:4 * h + 3],
                AF.Ln, bias=sconst[0:1, 0:1], scale=1.0))
            marks[("chs", h)] = E.op(
                "scalar", lambda e, h=h: e.activation(
                    chtmp[0:1, 4 * h + 2:4 * h + 3], chtmp[0:1, 4 * h + 3:4 * h + 4],
                    AF.Exp, bias=sconst[0:1, 2:3], scale=0.5),
                inc=s_ch)
            E.wait("vector", [marks[("chs", h)]])
            E.op("vector", lambda e, h=h: e.tensor_scalar_mul(
                chtmp[0:1, 4 * h + 3:4 * h + 4],
                chtmp[0:1, 4 * h + 2:4 * h + 3], -1.0))
            marks[("chn", h)] = E.op(
                "vector", lambda e: e.memset(junk[0:1, 12:13], 0.0),
                inc=s_ch)
            marks[("cbps", h)] = E.op(
                "tensor", lambda e, h=h: e.matmul(
                    ps_cb[:], onesrowf[0:1, :], chtmp[0:1, 4 * h + 3:4 * h + 4],
                    start=True, stop=True),
                waits=[marks[("chn", h)], mget(("cbcp", h - 1))],
                inc=s_ch)
            E.wait("vector", [marks[("cbps", h)]])
            marks[("cbcp", h)] = E.op(
                "vector", lambda e, h=h: e.tensor_copy(cbias[:, h:h + 1], ps_cb[:]),
                inc=s_ch)
        if debug:
            E.op("sync", lambda e: e.dma_start(out=dbg["dbg_m2"], in_=m2parts[:]),
                 waits=[mget(("cbcp", NH - 1))], inc=s_cs, k=16)
            E.op("sync", lambda e: e.dma_start(out=dbg["dbg_cb"], in_=cbias[:]),
                 inc=s_cs, k=16)
            E.op("sync", lambda e: e.dma_start(out=dbg["dbg_ch"], in_=chtmp[:]),
                 inc=s_cs, k=16)
        marks["barC"] = barrier()

    if phases >= 3:
        E.run_phase("phC", phaseC)
    pC.close()

    pD = st.enter_context(contextlib.ExitStack())
    fab = pD.enter_context(nc.sbuf_tensor("fab", [128, 2 * MYR], F32))
    fxb = [pD.enter_context(nc.sbuf_tensor(f"fxb{i}", [128, MYR], F32)) for i in range(2)]
    fxn = [pD.enter_context(nc.sbuf_tensor(f"fxn{i}", [128, MYR], BF16)) for i in range(2)]
    fttmp = pD.enter_context(nc.sbuf_tensor("fttmp", [128, MYR], F32))
    upb = pD.enter_context(nc.sbuf_tensor("upb", [128, 2 * 1024], BF16))
    dnb = pD.enter_context(nc.sbuf_tensor("dnb", [128, 8 * 256], BF16))
    midT = pD.enter_context(nc.sbuf_tensor("midT", [128, 8 * MYR], BF16))
    fstage = pD.enter_context(nc.sbuf_tensor("fstage", [128, 256], BF16))
    ps_fm = [pD.enter_context(nc.psum_tensor(f"ps_fm{i}", [128, 512], F32)) for i in range(2)]
    ps_fd = [pD.enter_context(nc.psum_tensor(f"ps_fd{i}", [128, 256], F32)) for i in range(2)]
    ps_fab = pD.enter_context(nc.psum_tensor("ps_fab", [128, 512], F32))

    def phaseD():
        phase_wait(marks["barC"])
        for j, (rsrc, rbase) in enumerate([(afrow, 64), (bfrow[0:1, :], 0)]):
            for n4 in range(4):
                marks[("fabps", j, n4)] = E.op(
                    "tensor", lambda e, r=rsrc, rb=rbase, n4=n4: e.matmul(
                        ps_fab[:], onesrowf[rb:rb + 1, :],
                        r[0:1, 512 * n4:512 * (n4 + 1)],
                        start=True, stop=True),
                    waits=[mget(("rowcp", 1, 15)), (s_const, 112),
                           mget(("fabcp", j, n4 - 1)) if n4 else
                           mget(("fabcp", j - 1, 3))],
                    inc=s_fab)
                E.wait("vector", [marks[("fabps", j, n4)]])
                marks[("fabcp", j, n4)] = E.op(
                    "vector", lambda e, j=j, n4=n4: e.tensor_copy(
                        fab[:, MYR * j + 512 * n4:MYR * j + 512 * (n4 + 1)],
                        ps_fab[:]),
                    inc=s_fab)
        fm_idx = 0
        fd_idx = 0
        for ex in range(16):
            for hf in range(2):
                ci = 2 * ex + hf
                marks[("fx", ex, hf)] = E.op(
                    "sync", lambda e, b=fxb[hf], ci=ci: e.dma_start(
                        out=b[:], in_=xT_ffn[128 * ci:128 * ci + 128, :]),
                    waits=[mget(("fn", ex - 1, hf))],
                    inc=s_fxp[hf], k=16)
                E.wait("vector", [marks[("fx", ex, hf)],
                                  mget(("fabcp", 1, 3)),
                                  mget(("fmex", ex - 1))])
                E.op("vector", lambda e, b=fxb[hf]: e.tensor_mul(
                    fttmp[:], b[:], fab[:, 0:MYR]))
                marks[("fn", ex, hf)] = E.op(
                    "vector", lambda e, b=fxn[hf]: e.tensor_add(
                        b[:], fttmp[:], fab[:, MYR:2 * MYR]),
                    inc=s_fn)
            marks[("fwu", ex)] = E.op(
                "sync", lambda e, ex=ex: e.dma_start(
                    out=upb[:], in_=up_bf[128 * ex:128 * (ex + 1), :]),
                waits=[mget(("cast", "up")), mget(("fmex", ex - 1))],
                inc=s_fwu, k=16)
            for mc in range(8):
                for rc in range(4):
                    ps = ps_fm[fm_idx % 2]
                    E.wait("tensor", [marks[("fwu", ex)], marks[("fn", ex, 1)],
                                      mget(("fg", fm_idx - 2))])
                    mvv = None
                    for hf in range(2):
                        mvv = E.op(
                            "tensor", lambda e, ps=ps, hf=hf, mc=mc, rc=rc: e.matmul(
                                ps[:], upb[:, 1024 * hf + 128 * mc:
                                            1024 * hf + 128 * (mc + 1)],
                                fxn[hf][:, 512 * rc:512 * (rc + 1)],
                                start=(hf == 0), stop=(hf == 1)),
                            inc=s_fm if hf == 1 else None)
                    marks[("fm", fm_idx)] = mvv
                    E.wait("scalar", [marks[("fm", fm_idx)],
                                      (mget(("fdex", ex - 1))
                                       if (mc == 0 and rc == 0) else None)])
                    marks[("fg", fm_idx)] = E.op(
                        "scalar", lambda e, ps=ps, mc=mc, rc=rc: e.activation(
                            midT[:, MYR * mc + 512 * rc:MYR * mc + 512 * (rc + 1)],
                            ps[:], AF.Gelu, bias=sconst[:, 0:1], scale=1.0),
                        inc=s_fg)
                    fm_idx += 1
            marks[("fmex", ex)] = marks[("fm", fm_idx - 1)]
            marks[("fwd", ex)] = E.op(
                "sync", lambda e, ex=ex: e.dma_start(
                    out=dnb[:], in_=down_bf[128 * ex:128 * (ex + 1), :]),
                waits=[mget(("cast", "down")), mget(("fdex", ex - 1))],
                inc=s_fwd, k=16)
            for rt in range(16):
                ps = ps_fd[fd_idx % 2]
                E.wait("tensor", [marks[("fwd", ex)], marks[("fg", fm_idx - 1)],
                                  mget(("fe", fd_idx - 2))])
                mvv = None
                for mc in range(8):
                    mvv = E.op(
                        "tensor", lambda e, ps=ps, mc=mc, rt=rt: e.matmul(
                            ps[:], midT[:, MYR * mc + 128 * rt:
                                         MYR * mc + 128 * (rt + 1)],
                            dnb[:, 256 * mc:256 * (mc + 1)],
                            start=(mc == 0), stop=(mc == 7)),
                        inc=s_fd if mc == 7 else None)
                marks[("fd", fd_idx)] = mvv
                E.wait("vector", [marks[("fd", fd_idx)], mget(("fs", fd_idx - 1))])
                marks[("fe", fd_idx)] = E.op(
                    "vector", lambda e, ps=ps: e.tensor_copy(fstage[:], ps[:]),
                    inc=s_fe)
                E.wait("sync", [marks[("fe", fd_idx)]])
                marks[("fs", fd_idx)] = E.op(
                    "sync", lambda e, ex=ex, rt=rt: e.dma_start(
                        out=ffn_d[128 * rt:128 * (rt + 1), 256 * ex:256 * (ex + 1)],
                        in_=fstage[:]),
                    inc=s_fs, k=16)
                fd_idx += 1
            marks[("fdex", ex)] = marks[("fd", fd_idx - 1)]
        marks["barD"] = barrier()

    if phases >= 4:
        E.run_phase("phD", phaseD)
    pD.close()

    pE = st.enter_context(contextlib.ExitStack())
    kts = [pE.enter_context(nc.sbuf_tensor(f"kts{i}", [128, S], BF16)) for i in range(2)]
    vts = [pE.enter_context(nc.sbuf_tensor(f"vts{i}", [128, S], BF16)) for i in range(2)]
    qts = [pE.enter_context(nc.sbuf_tensor(f"qts{i}", [128, 512], BF16)) for i in range(2)]
    pT = pE.enter_context(nc.sbuf_tensor("pT", [128, 2 * 1024], BF16))
    pscr = pE.enter_context(nc.sbuf_tensor("pscr", [128, 1024], BF16))
    headsT = pE.enter_context(nc.sbuf_tensor("headsT", [128, NH * 512], BF16))
    dnrow = pE.enter_context(nc.sbuf_tensor("dnrow", [1, 512], F32))
    rcrow = pE.enter_context(nc.sbuf_tensor("rcrow", [1, 512], F32))
    rcrowb = pE.enter_context(nc.sbuf_tensor("rcrowb", [1, 512], BF16))
    bc_sb = pE.enter_context(nc.sbuf_tensor("bc_sb", [128, 512], F32))
    wod = [pE.enter_context(nc.sbuf_tensor(f"wod{i}", [128, 16 * 512], BF16)) for i in range(2)]
    rsrow = pE.enter_context(nc.sbuf_tensor("rsrow", [128, 4 * D], BF16))
    axr = pE.enter_context(nc.sbuf_tensor("axr", [128, 1024], F32))
    aat = pE.enter_context(nc.sbuf_tensor("aat", [128, 1024], BF16))
    aff = pE.enter_context(nc.sbuf_tensor("aff", [128, 1024], BF16))
    ao1 = pE.enter_context(nc.sbuf_tensor("ao1", [128, 1024], F32))
    ao2 = [pE.enter_context(nc.sbuf_tensor(f"ao2_{i}", [128, 1024], F32)) for i in range(2)]
    ps_S = [pE.enter_context(nc.psum_tensor(f"ps_S{i}", [128, 1024], F32)) for i in range(2)]
    ps_pv = pE.enter_context(nc.psum_tensor("ps_pv", [128, 512], F32))
    ps_dnbc = pE.enter_context(nc.psum_tensor("ps_dnbc", [128, 512], F32))
    ps_wo = [pE.enter_context(nc.psum_tensor(f"ps_wo{i}", [128, 512], F32)) for i in range(2)]

    def phaseE():
        phase_wait(marks["barD"])
        g_idx = 0
        wo_idx = 0
        hsq = 0
        for s in range(NSLAB):
            kext = SLAB * (s + 1)
            G = kext // 256
            for h in range(NH):
                kb, vb, qb = kts[hsq % 2], vts[hsq % 2], qts[hsq % 2]
                E.op("sync", lambda e, b=kb, h=h, k2=kext: e.dma_start(
                    out=b[:, 0:k2], in_=kT_d[128 * h:128 * h + 128, 0:k2]),
                    waits=[mget(("pvh", hsq - 2)), mget(("qs", NWIN - 1, NH - 1, 1))],
                    inc=s_kvp[hsq % 2], k=16)
                E.op("sync", lambda e, b=vb, h=h, k2=kext: e.dma_start(
                    out=b[:, 0:k2], in_=v_d[:, 4096 * h:4096 * h + k2]),
                    waits=[mget(("vs", NWIN - 1, 7, 1))],
                    inc=s_kvp[hsq % 2], k=16)
                marks[("kv", hsq)] = E.op(
                    "sync", lambda e, b=qb, h=h, s=s: e.dma_start(
                        out=b[:], in_=qT_d[128 * h:128 * h + 128,
                                           SLAB * s:SLAB * (s + 1)]),
                    inc=s_kvp[hsq % 2], k=16)
                for g in range(G):
                    ps = ps_S[g_idx % 2]
                    E.wait("tensor", [marks[("kv", hsq)], mget(("exg", g_idx - 2))])
                    mvv = None
                    for t2 in range(2):
                        kt = 2 * g + t2
                        mvv = E.op(
                            "tensor", lambda e, ps=ps, kb=kb, qb=qb, kt=kt, t2=t2:
                                e.matmul(ps[:, 512 * t2:512 * (t2 + 1)],
                                         kb[:, 128 * kt:128 * kt + 128], qb[:],
                                         start=True, stop=True),
                            inc=s_Sg if t2 == 1 else None)
                    marks[("Sg", g_idx)] = mvv
                    diag = (g >= G - 2)
                    E.wait("scalar", [marks[("Sg", g_idx)], mget(("cbcp", NH - 1)),
                                      mget(("pgg", g_idx - 2)),
                                      mget(("mkg", g_idx - 1)) if diag else None])
                    marks[("exg", g_idx)] = E.op(
                        "scalar", lambda e, ps=ps, h=h, d2=diag, gi=g_idx:
                            e.activation(
                                pscr[:] if d2 else
                                pT[:, 1024 * (gi % 2):1024 * (gi % 2 + 1)],
                                ps[:], AF.Exp, bias=cbias[:, h:h + 1], scale=1.0),
                        inc=s_ex)
                    if diag:
                        dofs = 2 * g - 4 * s
                        E.wait("vector", [marks[("exg", g_idx)], (s_const, 112)])
                        for t2 in range(2):
                            mi = dofs + t2
                            E.op("vector", lambda e, gi=g_idx, t2=t2, mi=mi:
                                 e.tensor_mul(
                                     pT[:, 1024 * (gi % 2) + 512 * t2:
                                        1024 * (gi % 2) + 512 * (t2 + 1)],
                                     pscr[:, 512 * t2:512 * (t2 + 1)],
                                     masks[:, 512 * mi:512 * (mi + 1)]))
                        marks[("mkg", g_idx)] = E.op(
                            "vector", lambda e: e.memset(junk[0:1, 6:7], 0.0),
                            inc=s_mk)
                    pvw = [marks[("mkg", g_idx)] if diag else marks[("exg", g_idx)],
                           (s_const, 112)]
                    if g == 0:
                        pvw += [mget(("hTh", hsq - 1)), mget(("be", hsq - 1))]
                    E.wait("tensor", pvw)
                    mvv = None
                    for t2 in range(2):
                        kt = 2 * g + t2
                        E.op("tensor", lambda e, vb=vb, kt=kt, gi=g_idx, t2=t2:
                             e.matmul(
                                 ps_pv[:], vb[:, 128 * kt:128 * kt + 128],
                                 pT[:, 1024 * (gi % 2) + 512 * t2:
                                    1024 * (gi % 2) + 512 * (t2 + 1)],
                                 start=(kt == 0), stop=(kt == 2 * G - 1)))
                        mvv = E.op(
                            "tensor", lambda e, gi=g_idx, t2=t2, kt=kt: e.matmul(
                                ps_dnbc[0:1, :], onescol[:],
                                pT[:, 1024 * (gi % 2) + 512 * t2:
                                   1024 * (gi % 2) + 512 * (t2 + 1)],
                                start=(kt == 0), stop=(kt == 2 * G - 1)),
                            inc=s_pg if t2 == 1 else None)
                    marks[("pgg", g_idx)] = mvv
                    g_idx += 1
                marks[("pvh", hsq)] = marks[("pgg", g_idx - 1)]
                E.wait("vector", [marks[("pvh", hsq)]])
                E.op("vector", lambda e: e.tensor_copy(dnrow[:], ps_dnbc[0:1, :]))
                marks[("rcp", hsq)] = E.op(
                    "vector", lambda e: e.reciprocal(rcrow[:], dnrow[:]), inc=s_dc)
                E.wait("vector", [marks[("rcp", hsq)]])
                E.op("vector", lambda e: e.tensor_copy(rcrowb[:], rcrow[:]))
                marks[("rc", hsq)] = E.op(
                    "vector", lambda e: e.memset(junk[0:1, 14:15], 0.0),
                    inc=s_dc)
                marks[("bch", hsq)] = E.op(
                    "tensor", lambda e: e.matmul(ps_dnbc[:], onesrow[:], rcrowb[:],
                                                 start=True, stop=True),
                    waits=[marks[("rc", hsq)]],
                    inc=s_bc)
                E.wait("vector", [marks[("bch", hsq)]])
                marks[("be", hsq)] = E.op(
                    "vector", lambda e: e.tensor_copy(bc_sb[:], ps_dnbc[:]), inc=s_be)
                E.wait("vector", [mget(("woh", s - 1))])
                marks[("hTh", hsq)] = E.op(
                    "vector", lambda e, h=h: e.tensor_mul(
                        headsT[:, 512 * h:512 * (h + 1)], ps_pv[:], bc_sb[:]),
                    inc=s_hT)
                hsq += 1
            for dc in range(8):
                wb = wod[dc % 2]
                marks[("wd", s, dc)] = E.op(
                    "sync", lambda e, b=wb, dc=dc: e.dma_start(
                        out=b[:], in_=wo_bf[128 * dc:128 * (dc + 1), :]),
                    waits=[mget(("cast", "wo")),
                           mget(("wodc", s, dc - 2)) if dc >= 2 else
                           mget(("wodc", s - 1, dc + 6))],
                    inc=s_wdp[dc % 2], k=16)
                for rt in range(4):
                    ps = ps_wo[wo_idx % 2]
                    E.wait("tensor", [marks[("wd", s, dc)], marks[("hTh", hsq - 1)],
                                      mget(("we", wo_idx - 2))])
                    mvv = None
                    for hh in range(NH):
                        mvv = E.op(
                            "tensor", lambda e, ps=ps, b=wb, hh=hh, rt=rt: e.matmul(
                                ps[:], headsT[:, 512 * hh + 128 * rt:
                                              512 * hh + 128 * (rt + 1)],
                                b[:, 512 * hh:512 * (hh + 1)],
                                start=(hh == 0), stop=(hh == NH - 1)),
                            inc=s_wo if hh == NH - 1 else None)
                    marks[("wo", wo_idx)] = mvv
                    E.wait("vector", [marks[("wo", wo_idx)],
                                      mget(("rw", s - 1, rt)) if dc == 0 else None])
                    marks[("we", wo_idx)] = E.op(
                        "vector", lambda e, ps=ps, rt=rt, dc=dc: e.tensor_copy(
                            rsrow[:, D * rt + 512 * dc:D * rt + 512 * (dc + 1)],
                            ps[:]),
                        inc=s_we)
                    wo_idx += 1
                marks[("wodc", s, dc)] = marks[("wo", wo_idx - 1)]
            marks[("woh", s)] = marks[("wo", wo_idx - 1)]
            for rt in range(4):
                E.wait("sync", [marks[("we", wo_idx - 1)]])
                marks[("rw", s, rt)] = E.op(
                    "sync", lambda e, s=s, rt=rt: e.dma_start(
                        out=rs_in_s[s][128 * rt:128 * (rt + 1), :],
                        in_=rsrow[:, D * rt:D * (rt + 1)]),
                    inc=s_rw, k=16)
            E.wait("gpsimd", [marks[("rw", s, 3)]])
            marks[("cc", s)] = E.op(
                "gpsimd", lambda e, s=s: e.collective_compute(
                    "ReduceScatter", ALU.add,
                    replica_groups=[[0, 1], [2, 3], [4, 5], [6, 7]],
                    ins=[rs_in_s[s].opt()],
                    outs=[rs_out_s[s].opt()]),
                inc=s_cc)
            for rt2 in range(2):
                for cc4 in range(4):
                    r0 = 256 * s + 128 * rt2
                    c0 = 1024 * cc4
                    ai = (s * 2 + rt2) * 4 + cc4
                    E.op("sync", lambda e, r0=r0, c0=c0: e.dma_start(
                        out=axr[:], in_=x_res[r0:r0 + 128, c0:c0 + 1024]),
                        waits=[mget(("avl", ai - 1))], inc=s_alp[0], k=16)
                    E.op("sync", lambda e, s=s, rt2=rt2, c0=c0: e.dma_start(
                        out=aat[:], in_=rs_out_s[s][128 * rt2:128 * (rt2 + 1),
                                                    c0:c0 + 1024]),
                        waits=[marks[("cc", s)]], inc=s_alp[1], k=16)
                    marks[("al", ai)] = E.op(
                        "sync", lambda e, r0=r0, c0=c0: e.dma_start(
                            out=aff[:], in_=ffn_d[r0:r0 + 128, c0:c0 + 1024]),
                        waits=[mget(("fs", 255))], inc=s_alp[2], k=16)
                    E.wait("vector", [(s_alp[0], marks[("al", ai)][1]),
                                      (s_alp[1], marks[("al", ai)][1]),
                                      marks[("al", ai)],
                                      mget(("odl", ai - 2))])
                    E.op("vector", lambda e: e.tensor_add(ao1[:], axr[:], aat[:]))
                    marks[("avl", ai)] = E.op(
                        "vector", lambda e, ai=ai: e.tensor_add(
                            ao2[ai % 2][:], ao1[:], aff[:]),
                        inc=s_av)
                    E.wait("sync", [marks[("avl", ai)]])
                    marks[("odl", ai)] = E.op(
                        "sync", lambda e, r0=r0, c0=c0, ai=ai: e.dma_start(
                            out=out[r0:r0 + 128, c0:c0 + 1024], in_=ao2[ai % 2][:]),
                        inc=s_odp[ai % 2], k=16)
        E.wait("sync", [marks[("odl", NSLAB * 8 - 2)], marks[("odl", NSLAB * 8 - 1)]])
        E.op("sync", lambda e: e.nop())

    if phases >= 5:
        E.run_phase("phE", phaseE)
    pE.close()

    if phases < 5:
        s_dum = E.sem(st, "s_dum")

        def phaseDummy():
            n = None
            for t in range(16):
                n = E.op("sync", lambda e, t=t: e.dma_start(
                    out=out[128 * t:128 * (t + 1), :],
                    in_=x_res[128 * t:128 * (t + 1), :]),
                    inc=s_dum, k=16)
            E.wait("sync", [n])
            E.op("sync", lambda e: e.nop())

        E.run_phase("phDummy", phaseDummy)

    if debug:
        s_dbg = E.sem(st, "s_dbg")

        def phaseDbg():
            for rr, ncol in ((0, S), (32, S), (64, MYR)):
                E.op("sync", lambda e, rr=rr, ncol=ncol: e.dma_start(
                    out=dbg["dbg_rows"][rr:rr + 1, 0:ncol],
                    in_=rows_sb[rr:rr + 1, 0:ncol]),
                    waits=[mget(("odl", NSLAB * 8 - 1))], inc=s_dbg, k=16)
            for nm, src in ([("dbg_qT", qT_d), ("dbg_kT", kT_d), ("dbg_v", v_d),
                             ("dbg_ffn", ffn_d), ("dbg_rsin", rs_in_s[0])]
                            if phases >= 5 else []):
                E.op("sync", lambda e, d2=dbg[nm], s2=src: e.dma_start(
                    out=d2, in_=s2), inc=s_dbg, k=16)
            E.wait("sync", [(s_dbg, s_dbg.n)])
            E.op("sync", lambda e: e.nop())

        E.run_phase("phDbg", phaseDbg)

    st.close()
    return nc


_NC_CACHE = {}


def _get_nc(debug=False, phases=5):
    key = (debug, phases)
    if key not in _NC_CACHE:
        _NC_CACHE[key] = build_nc(debug, phases)
    return _NC_CACHE[key]


def _myrows(eta):
    return np.concatenate(
        [np.arange(512 * s + 256 * eta, 512 * s + 256 * eta + 256) for s in range(8)])


def make_in_maps(inputs):
    x = np.asarray(inputs["L0_attn_hidden_states"], dtype=np.float32)
    wq = np.asarray(inputs["L0_attn_wq"], dtype=np.float32)
    wk = np.asarray(inputs["L0_attn_wk"], dtype=np.float32)
    wv = np.asarray(inputs["L0_attn_wv"], dtype=np.float32)
    wo = np.asarray(inputs["L0_attn_wo"], dtype=np.float32)
    up = np.asarray(inputs["L0_ffn_up"], dtype=np.float32).reshape(16 * 256, 1024)
    down = np.asarray(inputs["L0_ffn_down"], dtype=np.float32).reshape(16 * 1024, 256)
    perm = np.asarray(inputs["L0_ffn_perm"], dtype=np.int64)
    _ = (inputs["L0_attn_kq1"], inputs["L0_attn_kq2"], inputs["L0_attn_kk1"],
         inputs["L0_attn_kk2"], inputs["L0_attn_scalingD"],
         inputs["L0_attn_interaction_k"], inputs["L0_attn_scalingD2"])

    xp = x[:, :, perm]
    wqp = wq[perm, :]
    wkp = wk[perm, :]
    wvp = wv[perm, :]
    wop = wo[:, perm]

    def lay_qk(wh):
        return np.ascontiguousarray(
            wh.reshape(32, 128, 16, 128).transpose(2, 1, 0, 3).reshape(HS, D))

    def lay_v(wh):
        return np.ascontiguousarray(
            wh.reshape(32, 128, 8, 256).transpose(2, 1, 0, 3).reshape(1024, 8192))

    def lay_o(wh):
        return np.ascontiguousarray(
            wh.reshape(16, 128, 8, 512).transpose(2, 1, 0, 3).reshape(1024, 8192))

    up_l = np.ascontiguousarray(
        up.reshape(16, 2, 128, 1024).transpose(0, 2, 1, 3).reshape(2048, 2048))
    down_l = np.ascontiguousarray(
        down.reshape(16, 8, 128, 256).transpose(0, 2, 1, 3).reshape(2048, 2048))

    in_maps = []
    for c in range(8):
        b, eta = c // 2, c % 2
        rows = _myrows(eta)
        xb = np.ascontiguousarray(xp[b])
        xres = np.ascontiguousarray(xb[rows])
        in_maps.append({
            "xT": np.ascontiguousarray(xb.T),
            "xrow": xb,
            "x_res": xres,
            "xT_ffn": np.ascontiguousarray(xres.T),
            "wq": lay_qk(wqp[:, HS * eta:HS * (eta + 1)]),
            "wk": lay_qk(wkp[:, HS * eta:HS * (eta + 1)]),
            "wv": lay_v(wvp[:, HS * eta:HS * (eta + 1)]),
            "wo": lay_o(wop[HS * eta:HS * (eta + 1), :]),
            "up": up_l,
            "down": down_l,
        })
    return in_maps, perm


def kernel(**inputs):
    in_maps, perm = make_in_maps(inputs)
    nc = _get_nc(False)
    res = run_bass_kernel_spmd(nc, in_maps, core_ids=list(range(8)))
    outp = np.empty((4, S, D), dtype=np.float32)
    for c in range(8):
        b, eta = c // 2, c % 2
        outp[b, _myrows(eta)] = np.asarray(res.results[c]["out"])
    final = np.empty_like(outp)
    final[:, :, perm] = outp
    return final
